# version 09
# baseline (speedup 1.0000x reference)
import numpy as np
import ml_dtypes
import concourse.bass as bass
import concourse.mybir as mybir
from concourse.bass_utils import run_bass_kernel_spmd

F32 = mybir.dt.float32
BF16 = mybir.dt.bfloat16
AF = mybir.ActivationFunctionType
ALU = mybir.AluOpType
AX = mybir.AxisListType

L = 2048
D = 2048
E = 8
H = 1408
HS = 2816
RK = 64
NS = 16
TC = 512
NTT = 4
DCN = 16
HCN = 11
EPS = 1e-5

ENGINES = ("pe", "ve", "se", "gp", "sy")
DMA_ENG = ("gp", "sy")


NQ = 12


class Sched:

    def __init__(self, nc):
        self.nc = nc
        self.ops = {e: [] for e in ENGINES}
        self.sems = {e: nc.alloc_semaphore("sem_" + e) for e in ENGINES}
        for q in ("sy", "gp", "se"):
            for i in range(NQ):
                self.sems[(q, i)] = nc.alloc_semaphore(f"dq_{q}{i}")
        self.counts = {k: 0 for k in self.sems}
        self.dma_rr = {"sy": 0, "gp": 0, "se": 0}

    def op(self, eng, fn, waits=(), sig=False, inc=None):
        ev = None
        waits = tuple(w for w in waits if w is not None)
        if sig:
            if inc is None:
                inc = 16 if eng in DMA_ENG else 1
            if inc == 16:
                i = self.dma_rr[eng]
                self.dma_rr[eng] = (i + 1) % NQ
                key = (eng, i)
            else:
                key = eng
            self.counts[key] += inc
            ev = (key, self.counts[key])
        self.ops[eng].append((fn, waits, ev))
        return ev

    def emit(self, blk):
        mk = {"pe": blk.tensor, "ve": blk.vector, "se": blk.scalar,
              "gp": blk.gpsimd, "sy": blk.sync}
        for eng in ENGINES:
            mk[eng](self._mk_body(eng))

    def _mk_body(self, eng):
        sems = self.sems
        ops = self.ops[eng]

        def _body(e):
            last_waits = {}
            prev = {}
            for fn, waits, ev in ops:
                for (wkey, wval) in waits:
                    if last_waits.get(wkey, -1) >= wval:
                        continue
                    e.wait_ge(sems[wkey], wval)
                    last_waits[wkey] = wval
                ins = fn(e)
                if ev is not None:
                    key = ev[0]
                    ins.then_inc(sems[key], ev[1] - prev.get(key, 0))
                    prev[key] = ev[1]
        return _body


def build_nc(n_chunks=4, n_experts=8, use_shared=True, phases=("f1", "mb", "f2"),
             use_scratch=True, debug=False):
    nc = bass.Bass("TRN2", target_bir_lowering=False)

    x_in = nc.dram_tensor("x_in", [L, D], F32, kind="ExternalInput")
    gwT = nc.dram_tensor("gwT", [128, DCN * 16], F32, kind="ExternalInput")
    wg = nc.dram_tensor("wg", [E, D, H], F32, kind="ExternalInput")
    wu = nc.dram_tensor("wu", [E, D, H], F32, kind="ExternalInput")
    wd = nc.dram_tensor("wd", [E, H, D], F32, kind="ExternalInput")
    shg = nc.dram_tensor("shg", [128, DCN], F32, kind="ExternalInput")
    shwg = nc.dram_tensor("shwg", [D, HS], F32, kind="ExternalInput")
    shwu = nc.dram_tensor("shwu", [D, HS], F32, kind="ExternalInput")
    shwd = nc.dram_tensor("shwd", [HS, D], F32, kind="ExternalInput")
    wxT = nc.dram_tensor("wxT", [128, DCN * 96], F32, kind="ExternalInput")
    wdtT = nc.dram_tensor("wdtT", [RK, D], F32, kind="ExternalInput")
    bdt = nc.dram_tensor("bdt", [128, DCN], F32, kind="ExternalInput")
    alog = nc.dram_tensor("alog", [128, 256], F32, kind="ExternalInput")
    dvec = nc.dram_tensor("dvec", [128, DCN], F32, kind="ExternalInput")
    idf = nc.dram_tensor("idf", [128, 128], F32, kind="ExternalInput")
    selbc = nc.dram_tensor("selbc", [16, 16 * 128], F32, kind="ExternalInput")
    y_out = nc.dram_tensor("y_out", [L, D], F32, kind="ExternalOutput")
    if debug:
        dbg_xnT = nc.dram_tensor("dbg_xnT", [128, DCN * TC], BF16,
                                 kind="ExternalOutput")
        dbg_lgT = nc.dram_tensor("dbg_lgT", [128, 64], F32, kind="ExternalOutput")
        dbg_comb = nc.dram_tensor("dbg_comb", [128, 32], F32,
                                  kind="ExternalOutput")
        dbg_h = nc.dram_tensor("dbg_h", [128, HCN * TC], BF16,
                               kind="ExternalOutput")
        dbg_st = nc.dram_tensor("dbg_st", [128, 48], F32, kind="ExternalOutput")
    if use_scratch:
        scr = nc.dram_tensor("scr", [128, 680000], BF16)

    A = nc.alloc_sbuf_tensor
    x_bufs = [A(f"x_sb{i}", [128, NTT * D], F32) for i in range(2)]
    xnT_f = A("xnT_f", [128, DCN * TC], BF16)
    xnT_m = A("xnT_m", [128, DCN * TC], BF16)
    xn_tok = [A(f"xn_tok{i}", [128, D], BF16) for i in range(2)]
    junk = A("junk", [128, D], BF16)
    s_sb = A("s_sb", [128, HCN * TC], BF16)
    guring = [A(f"guring{i}", [128, 1024], BF16) for i in range(8)]
    wdring = [A(f"wdring{i}", [128, 1024], BF16) for i in range(4)]
    Bb = A("Bb", [128, 8 * TC], BF16)
    Cb = A("Cb", [128, 8 * TC], BF16)
    dtp_sb = A("dtp_sb", [RK, TC], BF16)
    B_sb = A("B_sb", [NS, TC], BF16)
    C_sb = A("C_sb", [NS, TC], BF16)
    sp1 = [A(f"sp1_{i}", [128, TC], F32) for i in range(2)]
    sp2 = [A(f"sp2_{i}", [128, TC], F32) for i in range(2)]
    dt_b = [A(f"dt_b{i}", [128, TC], F32) for i in range(2)]
    dtx_b = [A(f"dtx_b{i}", [128, TC], BF16) for i in range(2)]
    ab_b = [A(f"ab_b{i}", [128, TC], F32) for i in range(2)]
    bb_b = [A(f"bb_b{i}", [128, TC], BF16) for i in range(2)]
    h_b = [A(f"h_b{i}", [128, TC], BF16) for i in range(2)]
    tmp_b = [A(f"tmp_b{i}", [128, TC], BF16) for i in range(2)]
    y_b = [A(f"y_b{i}", [128, TC], F32) for i in range(2)]
    state = A("state", [128, 256], F32)
    wdtT_sb = A("wdtT_sb", [RK, D], BF16)
    wxT_sb = A("wxT_sb", [128, DCN * 96], BF16)
    gwT_sb = A("gwT_sb", [128, DCN * 16], BF16)
    selbc_sb = A("selbc_sb", [16, 16 * 128], BF16)
    idf_sb = A("idf_sb", [128, 128], F32)
    a_sb = A("a_sb", [128, 256], F32)
    bdt_sb = A("bdt_sb", [128, DCN], F32)
    d_sb = A("d_sb", [128, DCN], F32)
    shg_sb = A("shg_sb", [128, DCN], BF16)
    stats = A("stats", [128, 48], F32)
    lg_sb = A("lg_sb", [16, TC], F32)
    lgT = A("lgT", [128, NTT * 16], F32)
    comb05_p = [A(f"comb05_{i}", [128, NTT * 8], F32) for i in range(2)]
    sg05_p = [A(f"sg05_{i}", [128, NTT], F32) for i in range(2)]
    msk1 = A("msk1", [128, 8], F32)
    msk2 = A("msk2", [128, 8], F32)
    lg2 = A("lg2", [128, 8], F32)
    tcol = A("tcol", [128, 8], F32)
    cvals = [1.0, -1.0, EPS, -0.5, 0.5]
    cpool = A("cpool", [128, len(cvals)], F32)
    for i, vv in enumerate(cvals):
        nc.const_aps.register(F32, vv, cpool[:, i:i + 1])

    PSG = nc.alloc_psum_tensor("PSG", [128, 1024], F32)
    PSU = nc.alloc_psum_tensor("PSU", [128, 1024], F32)
    PSC = nc.alloc_psum_tensor("PSC", [128, 1024], F32)
    PSM = nc.alloc_psum_tensor("PSM", [128, 1024], F32)

    S = Sched(nc)

    def pe(fn, waits=(), sig=False):
        return S.op("pe", fn, waits, sig)

    def ve(fn, waits=(), sig=False):
        return S.op("ve", fn, waits, sig)

    def se(fn, waits=(), sig=False, inc=None):
        return S.op("se", fn, waits, sig, inc)

    def gp(fn, waits=(), sig=False, inc=None):
        return S.op("gp", fn, waits, sig, inc)

    def sy(fn, waits=(), sig=False):
        return S.op("sy", fn, waits, sig)

    for i, vv in enumerate(cvals):
        ve(lambda e, i=i, vv=vv: e.memset(cpool[:, i:i + 1], vv))
    ve(lambda e: e.memset(state[:], 0.0))
    sy(lambda e: e.dma_start(idf_sb[:], idf[:]), sig=True)
    sy(lambda e: e.dma_start(bdt_sb[:], bdt[:]), sig=True)
    sy(lambda e: e.dma_start(d_sb[:], dvec[:]), sig=True)
    ev_alog = sy(lambda e: e.dma_start(a_sb[:], alog[:]), sig=True)
    gp(lambda e: e.dma_start(wdtT_sb[:], wdtT[:]), sig=True)
    gp(lambda e: e.dma_start(wxT_sb[:], wxT[:]), sig=True)
    gp(lambda e: e.dma_start(gwT_sb[:], gwT[:]), sig=True)
    gp(lambda e: e.dma_start(selbc_sb[:], selbc[:]), sig=True)
    ev_setup_gp = gp(lambda e: e.dma_start(shg_sb[:], shg[:]), sig=True)
    ev_exp = se(lambda e: e.activation(a_sb[:], a_sb[:], AF.Exp),
                waits=[ev_alog], sig=True)
    ev_a = ve(lambda e: e.tensor_single_scalar(a_sb[:], a_sb[:], -1.0, ALU.mult),
              waits=[ev_exp], sig=True)

    ln_evT = [None, None]

    def layer_norm(x_ready, x_sb, xnT, sb):
        ev_sums = None
        for tt in range(NTT):
            ev_sums = ve(lambda e, tt=tt: e.reduce_sum(
                stats[:, sb + tt:sb + tt + 1], x_sb[:, tt * D:(tt + 1) * D], AX.X),
                waits=[x_ready], sig=(tt == NTT - 1))
        ev_sq = None
        for tt in range(NTT):
            ev_sq = se(lambda e, tt=tt: e.activation(
                junk[:], x_sb[:, tt * D:(tt + 1) * D], AF.Square,
                accum_out=stats[:, sb + 4 + tt:sb + 5 + tt]),
                waits=[x_ready], sig=(tt == NTT - 1))
        evg = gp(lambda e: e.tensor_scalar(stats[:, sb + 8:sb + 12], stats[:, sb:sb + 4],
                                           1.0 / D, None, ALU.mult),
                 waits=[ev_sums], sig=True, inc=1)
        evg = gp(lambda e: e.tensor_scalar(stats[:, sb + 16:sb + 20], stats[:, sb + 4:sb + 8],
                                           1.0 / D, None, ALU.mult),
                 waits=[ev_sq, evg], sig=True, inc=1)
        evg = gp(lambda e: e.tensor_tensor(stats[:, sb + 20:sb + 24], stats[:, sb + 8:sb + 12],
                                           stats[:, sb + 8:sb + 12], ALU.mult),
                 waits=[evg], sig=True, inc=1)
        ev_var = gp(lambda e: e.tensor_tensor(stats[:, sb + 12:sb + 16], stats[:, sb + 16:sb + 20],
                                              stats[:, sb + 20:sb + 24], ALU.subtract),
                    waits=[evg], sig=True, inc=1)
        evl = se(lambda e: e.activation(stats[:, sb + 12:sb + 16], stats[:, sb + 12:sb + 16], AF.Ln,
                                        bias=EPS), waits=[ev_var], sig=True)
        ev_rstd = se(lambda e: e.activation(stats[:, sb + 12:sb + 16], stats[:, sb + 12:sb + 16],
                                            AF.Exp, scale=-0.5), waits=[evl],
                     sig=True)
        ev_T = ln_evT
        ev_all = None
        for tt in range(NTT):
            sl = tt % 2
            ev_xn = ve(lambda e, tt=tt, sl=sl: e.tensor_scalar(
                xn_tok[sl][:], x_sb[:, tt * D:(tt + 1) * D],
                stats[:, sb + 8 + tt:sb + 9 + tt], stats[:, sb + 12 + tt:sb + 13 + tt],
                ALU.subtract, ALU.mult),
                waits=[ev_rstd, ev_var, ev_T[sl]], sig=True)
            for dc in range(DCN):
                ev_all = sy(lambda e, tt=tt, sl=sl, dc=dc: e.dma_start(
                    xnT[:, dc * TC + tt * 128: dc * TC + (tt + 1) * 128],
                    xn_tok[sl][:, dc * 128:(dc + 1) * 128], transpose=True),
                    waits=[ev_xn], sig=True)
            ev_T[sl] = ev_all
        return ev_all

    wstate = {"guse": [None] * 8, "guwb": [None] * 8,
              "wdse": [None] * 4, "wdwb": [None] * 4,
              "gu_next": 0, "wd_next": 0, "converted": {}, "scr_off": 0}

    def load_slab(ring, use_l, wb_l, slot, src_f32, key):
        waits = [use_l[slot], wb_l[slot]]
        g, k = src_f32.shape[1], src_f32.shape[2]
        cols = g * k
        dst = ring[slot][:, 0:cols].rearrange("p (g k) -> p g k", g=g)
        if use_scratch:
            if key not in wstate["converted"]:
                off = wstate["scr_off"]
                wstate["scr_off"] = off + cols
                ev = gp(lambda e: e.dma_start(dst, src_f32),
                        waits=waits, sig=True)
                wb = gp(lambda e, off=off, cols=cols: e.dma_start(
                    scr[:, off:off + cols], ring[slot][:, 0:cols]),
                    waits=[ev], sig=True)
                wstate["converted"][key] = (off, wb)
                wb_l[slot] = wb
                return ev
            off, wb = wstate["converted"][key]
            return sy(lambda e, off=off, cols=cols: e.dma_start(
                ring[slot][:, 0:cols], scr[:, off:off + cols]),
                waits=waits + [wb], sig=True)
        return gp(lambda e: e.dma_start(dst, src_f32),
                  waits=waits, sig=True)

    HQ_E = [(i, 1) for i in range(HCN)]

    pstate = {"psg": None, "psu": None, "psc": [None, None], "ret": None}

    def ffn(x_ready, call_key, par):
        x_sb = x_bufs[par]
        xnT = xnT_f
        comb05 = comb05_p[par]
        sg05 = sg05_p[par]
        psg_war = pstate["psg"]
        psu_war = pstate["psu"]
        psc_war = pstate["psc"]
        ev_xnT_all = layer_norm(x_ready, x_sb, xnT, 0)

        ev_lgmm = None
        for dc in range(DCN):
            ev_lgmm = pe(lambda e, dc=dc: e.matmul(
                PSC[0:16, 0:TC], gwT_sb[:, dc * 16:(dc + 1) * 16],
                xnT[:, dc * TC:(dc + 1) * TC],
                start=(dc == 0), stop=(dc == DCN - 1), skip_group_check=True),
                waits=[ev_xnT_all, ev_setup_gp, psc_war[0]],
                sig=(dc == DCN - 1))
        ev_lgcp = ve(lambda e: e.tensor_copy(lg_sb[:], PSC[0:16, 0:TC]),
                     waits=[ev_lgmm], sig=True)
        ev_lgT = None
        for tt in range(NTT):
            ev_lgT = pe(lambda e, tt=tt: e.transpose(
                PSC[:, TC + tt * 16: TC + (tt + 1) * 16],
                lg_sb[:, tt * 128:(tt + 1) * 128], idf_sb[0:16, 0:16]),
                waits=[ev_lgcp, psc_war[1]], sig=(tt == NTT - 1))
        ev_lgTc = ve(lambda e: e.tensor_copy(
            lgT[:, 0:NTT * 16], PSC[:, TC:TC + NTT * 16]),
            waits=[ev_lgT], sig=True)

        ev_sgmm = None
        for tt in range(NTT):
            for dc in range(DCN):
                ev_sgmm = pe(lambda e, tt=tt, dc=dc: e.matmul(
                    PSC[:, tt:tt + 1],
                    xnT[:, dc * TC + tt * 128: dc * TC + (tt + 1) * 128],
                    shg_sb[:, dc:dc + 1],
                    start=(tt == 0 and dc == 0),
                    stop=(tt == NTT - 1 and dc == DCN - 1),
                    skip_group_check=True),
                    waits=[ev_xnT_all, ev_lgcp],
                    sig=(tt == NTT - 1 and dc == DCN - 1))
        evsg = se(lambda e: e.activation(sg05[:], PSC[:, 0:NTT],
                                         AF.Sigmoid), waits=[ev_sgmm], sig=True)
        ev_sg05 = se(lambda e: e.mul(sg05[:], sg05[:], 0.5), waits=[evsg],
                     sig=True)

        ev_comb = None
        ev4_prev = None
        for tt in range(NTT):
            lgt8 = lgT[:, tt * 16: tt * 16 + 8]
            c0 = tcol[:, 0:1]
            c1 = tcol[:, 1:2]
            c2 = tcol[:, 2:3]
            c34 = tcol[:, 3:5]
            evp = ve(lambda e, lgt8=lgt8, c0=c0: e.reduce_max(c0, lgt8, AX.X),
                     waits=[ev_lgTc], sig=True)
            evp = ve(lambda e, lgt8=lgt8, c0=c0: e.tensor_scalar(
                msk1[:], lgt8, c0, None, ALU.is_equal), waits=[evp], sig=True)
            evp = ve(lambda e, lgt8=lgt8: e.scalar_tensor_tensor(
                lg2[:], msk1[:], -1e30, lgt8, ALU.mult, ALU.add),
                waits=[evp], sig=True)
            evp = ve(lambda e, c1=c1: e.reduce_max(c1, lg2[:], AX.X),
                     waits=[evp], sig=True)
            ev3 = ve(lambda e, c0=c0, c1=c1, c2=c2: e.tensor_tensor(
                c2, c0, c1, ALU.subtract), waits=[evp, ev4_prev], sig=True)
            se(lambda e, c2=c2, c34=c34: e.activation(c34[:, 0:1], c2, AF.Sigmoid),
               waits=[ev3])
            ev4a = se(lambda e, c2=c2, c34=c34: e.activation(
                c34[:, 1:2], c2, AF.Sigmoid, scale=-1.0), sig=True)
            ev4 = se(lambda e, c34=c34: e.mul(c34, c34, 0.5), waits=[ev4a],
                     sig=True)
            ev4_prev = ev4
            evp = ve(lambda e, c1=c1: e.tensor_scalar(
                msk2[:], lg2[:], c1, None, ALU.subtract), waits=[ev4, ev3],
                sig=True)
            evp = ve(lambda e: e.tensor_single_scalar(msk2[:], msk2[:], 0.0,
                                                      ALU.is_equal),
                     waits=[evp], sig=True)
            evp = ve(lambda e, c34=c34, tt=tt: e.tensor_scalar(
                comb05[:, tt * 8:(tt + 1) * 8], msk1[:], c34[:, 0:1], None,
                ALU.mult), waits=[evp], sig=True)
            ev_comb = ve(lambda e, c34=c34, tt=tt: e.scalar_tensor_tensor(
                comb05[:, tt * 8:(tt + 1) * 8], msk2[:], c34[:, 1:2],
                comb05[:, tt * 8:(tt + 1) * 8], ALU.mult, ALU.add),
                waits=[evp], sig=True)

        yield
        ev_PSG_free = list(psg_war) if psg_war else [None, None]
        ev_PSU_free = list(psu_war) if psu_war else [None, None]
        ev_PSC_free = [ev_sg05, ev_lgTc]

        def gu_pass(srcs, h_groups, skey):
            nonlocal ev_PSG_free, ev_PSU_free
            evs_h = []
            for gi, (h0, hn) in enumerate(h_groups):
                bank = gi % 2
                poff = bank * TC
                slabs = {}
                for dg in range(2):
                    for kind in range(2):
                        slot = wstate["gu_next"]
                        wstate["gu_next"] = (slot + 1) % 8
                        ev = load_slab(
                            guring, wstate["guse"], wstate["guwb"], slot,
                            srcs[kind](dg, h0, hn),
                            key=(skey, gi, dg, kind))
                        slabs[(dg, kind)] = (slot, ev)
                ev_g = ev_u = None
                for dc in range(DCN):
                    dg, di = dc // 8, dc % 8
                    for kind in range(2):
                        slot, ev_ld = slabs[(dg, kind)]
                        tgt = PSG if kind == 0 else PSU
                        war = (ev_PSG_free if kind == 0 else ev_PSU_free)[bank]
                        evm = pe(lambda e, slot=slot, di=di, dc=dc,
                                 tgt=tgt, poff=poff: e.matmul(
                            tgt[:, poff:poff + TC],
                            guring[slot][:, di * 128:(di + 1) * 128],
                            xnT[:, dc * TC:(dc + 1) * TC],
                            start=(dc == 0), stop=(dc == DCN - 1),
                            skip_group_check=True),
                            waits=[ev_ld, ev_xnT_all,
                                   war if dc == 0 else None],
                            sig=(dc == dg * 8 + 7))
                        if evm is not None:
                            if kind == 0:
                                ev_g = evm
                            else:
                                ev_u = evm
                            wstate["guse"][slot] = evm
                ev_s = se(lambda e, h0=h0, poff=poff: e.activation(
                    s_sb[:, (h0 % HCN) * TC:((h0 % HCN) + 1) * TC],
                    PSG[:, poff:poff + TC], AF.Silu), waits=[ev_g], sig=True)
                ev_PSG_free[bank] = ev_s
                ev_hm = ve(lambda e, h0=h0, poff=poff: e.tensor_tensor(
                    s_sb[:, (h0 % HCN) * TC:((h0 % HCN) + 1) * TC],
                    s_sb[:, (h0 % HCN) * TC:((h0 % HCN) + 1) * TC],
                    PSU[:, poff:poff + TC], ALU.mult),
                    waits=[ev_s, ev_u], sig=True)
                ev_PSU_free[bank] = ev_hm
                evs_h.append(ev_hm)
            return evs_h

        def down_pass(src_fn, nhc, evs_h, skey, evac_fn):
            nonlocal ev_PSC_free
            ng = (nhc + 1) // 2
            for db in range(4):
                for ttp in range(2):
                    ev_dmm = None
                    for hc2 in range(ng):
                        hc0 = hc2 * 2
                        hcnt = 2 if hc0 + 2 <= nhc else 1
                        slot = wstate["wd_next"]
                        wstate["wd_next"] = (slot + 1) % 4
                        ev_ld = load_slab(
                            wdring, wstate["wdse"], wstate["wdwb"], slot,
                            src_fn(hc0, hcnt, db),
                            key=(skey, db, hc2))
                        ev_slab = None
                        for hl in range(hcnt):
                            hc = hc0 + hl
                            for tl in range(2):
                                tt = ttp * 2 + tl
                                evm = pe(lambda e, slot=slot, hl=hl, hc=hc,
                                         tt=tt, tl=tl, nhc=nhc: e.matmul(
                                    PSC[:, tl * 512:(tl + 1) * 512],
                                    s_sb[:, hc * TC + tt * 128:
                                         hc * TC + (tt + 1) * 128],
                                    wdring[slot][:, hl * 512:(hl + 1) * 512],
                                    start=(hc == 0), stop=(hc == nhc - 1),
                                    skip_group_check=True),
                                    waits=[ev_ld, evs_h[hc],
                                           ev_PSC_free[tl] if hc == 0 else None],
                                    sig=(hl == hcnt - 1 and tl == 1))
                                if evm is not None:
                                    ev_slab = evm
                        wstate["wdse"][slot] = ev_slab
                        ev_dmm = ev_slab
                    for tl in range(2):
                        ev_PSC_free[tl] = evac_fn(ttp * 2 + tl, tl, db, ev_dmm)

        for ex in range(n_experts):
            def gsrc(dg, h0, hn, ex=ex):
                return wg[ex, dg * 1024:(dg + 1) * 1024,
                          h0 * 128:(h0 + hn) * 128].rearrange(
                    "(g p) k -> p g k", p=128)

            def usrc(dg, h0, hn, ex=ex):
                return wu[ex, dg * 1024:(dg + 1) * 1024,
                          h0 * 128:(h0 + hn) * 128].rearrange(
                    "(g p) k -> p g k", p=128)
            evs_h = gu_pass((gsrc, usrc), HQ_E, (call_key, "gu", ex))

            def dsrc(hc0, hcnt, db, ex=ex):
                return wd[ex, hc0 * 128:(hc0 + hcnt) * 128,
                          db * 512:(db + 1) * 512].rearrange(
                    "(g p) k -> p g k", p=128)
            def evac(tt, tl, db, ev_dmm, ex=ex):
                return ve(lambda e, tt=tt, tl=tl, db=db, ex=ex:
                          e.scalar_tensor_tensor(
                    x_sb[:, tt * D + db * 512: tt * D + (db + 1) * 512],
                    PSC[:, tl * 512:(tl + 1) * 512],
                    comb05[:, tt * 8 + ex: tt * 8 + ex + 1],
                    x_sb[:, tt * D + db * 512: tt * D + (db + 1) * 512],
                    ALU.mult, ALU.add),
                    waits=[ev_dmm, ev_comb], sig=True)
            down_pass(dsrc, HCN, evs_h, (call_key, "wd", ex), evac)
            yield

        if use_shared:
            for half in range(2):
                hbase = half * 11

                def gsrc(dg, h0, hn):
                    return shwg[dg * 1024:(dg + 1) * 1024,
                                h0 * 128:(h0 + hn) * 128].rearrange(
                        "(g p) k -> p g k", p=128)

                def usrc(dg, h0, hn):
                    return shwu[dg * 1024:(dg + 1) * 1024,
                                h0 * 128:(h0 + hn) * 128].rearrange(
                        "(g p) k -> p g k", p=128)
                groups = [(hbase + a, b) for (a, b) in HQ_E]
                evs_h = gu_pass((gsrc, usrc), groups,
                                (call_key, "shgu", half))

                def dsrc(hc0, hcnt, db, hbase=hbase):
                    hh0 = hbase + hc0
                    return shwd[hh0 * 128:(hh0 + hcnt) * 128,
                                db * 512:(db + 1) * 512].rearrange(
                        "(g p) k -> p g k", p=128)
                def evac(tt, tl, db, ev_dmm):
                    return ve(lambda e, tt=tt, tl=tl, db=db:
                              e.scalar_tensor_tensor(
                        x_sb[:, tt * D + db * 512: tt * D + (db + 1) * 512],
                        PSC[:, tl * 512:(tl + 1) * 512],
                        sg05[:, tt:tt + 1],
                        x_sb[:, tt * D + db * 512: tt * D + (db + 1) * 512],
                        ALU.mult, ALU.add),
                        waits=[ev_dmm, ev_sg05], sig=True)
                down_pass(dsrc, 11, evs_h, (call_key, "shwd", half), evac)
                if half == 0:
                    yield
        pstate["psg"] = ev_PSG_free
        pstate["psu"] = ev_PSU_free
        pstate["psc"] = ev_PSC_free
        pstate["ret"] = ev_PSC_free[1]

    mstate = {"ret": None}

    def mamba(x_ready, par):
        x_sb = x_bufs[par]
        xnT = xnT_m
        ev_xnT_all = layer_norm(x_ready, x_sb, xnT, 24)
        yield
        ev_dtp = ev_B = ev_C = None
        for dc in range(DCN):
            ev_dtp = pe(lambda e, dc=dc: e.matmul(
                PSM[0:64, 0:TC], wxT_sb[:, dc * 96: dc * 96 + 64],
                xnT[:, dc * TC:(dc + 1) * TC], start=(dc == 0),
                stop=(dc == DCN - 1), skip_group_check=True),
                waits=[ev_xnT_all, ev_setup_gp], sig=(dc == DCN - 1))
        for dc in range(DCN):
            ev_B = pe(lambda e, dc=dc: e.matmul(
                PSM[0:16, TC:2 * TC], wxT_sb[:, dc * 96 + 64: dc * 96 + 80],
                xnT[:, dc * TC:(dc + 1) * TC], start=(dc == 0),
                stop=(dc == DCN - 1), skip_group_check=True),
                sig=(dc == DCN - 1))
        ev_dtp_sb = ve(lambda e: e.tensor_copy(dtp_sb[:], PSM[0:64, 0:TC]),
                       waits=[ev_dtp], sig=True)
        ev_B_sb = ve(lambda e: e.tensor_copy(B_sb[:], PSM[0:16, TC:2 * TC]),
                     waits=[ev_B], sig=True)
        for dc in range(DCN):
            ev_C = pe(lambda e, dc=dc: e.matmul(
                PSM[0:16, TC:2 * TC], wxT_sb[:, dc * 96 + 80: dc * 96 + 96],
                xnT[:, dc * TC:(dc + 1) * TC], start=(dc == 0),
                stop=(dc == DCN - 1), skip_group_check=True),
                waits=[ev_B_sb if dc == 0 else None], sig=(dc == DCN - 1))
        ev_C_sb = ve(lambda e: e.tensor_copy(C_sb[:], PSM[0:16, TC:2 * TC]),
                     waits=[ev_C], sig=True)
        pending = []

        def flush_yT():
            dc, par2, ev_y_last, ev_sp = pending.pop(0)
            tb = par2 * 512
            ev_yT = None
            for tt in range(NTT):
                ev_yT = pe(lambda e, tt=tt, par2=par2, tb=tb: e.transpose(
                    PSM[:, tb + tt * 128: tb + (tt + 1) * 128],
                    y_b[par2][:, tt * 128:(tt + 1) * 128], idf_sb[:]),
                    waits=[ev_y_last, ev_sp], sig=(tt == NTT - 1))
            ev_add = None
            for tt in range(NTT):
                ev_add = ve(lambda e, tt=tt, dc=dc, tb=tb: e.tensor_tensor(
                    x_sb[:, tt * D + dc * 128: tt * D + (dc + 1) * 128],
                    x_sb[:, tt * D + dc * 128: tt * D + (dc + 1) * 128],
                    PSM[:, tb + tt * 128: tb + (tt + 1) * 128], ALU.add),
                    waits=[ev_yT], sig=(tt == NTT - 1))
            ev_y_free[par2] = ev_add
            bank_free[par2] = ev_add
            mstate["last_add"] = ev_add

        bank_free = [ev_dtp_sb, ev_C_sb]
        ev_sp_free = [None, None]
        ev_ab_last = [None, None]
        ev_y_free = [None, None]
        ev_h_hist = [None, None]
        ev_final = None
        for nh in range(2):
            for i in range(8):
                n = nh * 8 + i
                for kind in range(2):
                    srcb = B_sb if kind == 0 else C_sb
                    dstt = Bb if kind == 0 else Cb
                    ev_src = ev_B_sb if kind == 0 else ev_C_sb
                    bank = kind
                    evm = pe(lambda e, n=n, srcb=srcb, bank=bank: e.matmul(
                        PSM[:, bank * 512: bank * 512 + TC],
                        selbc_sb[:, n * 128:(n + 1) * 128], srcb[:],
                        start=True, stop=True, skip_group_check=True),
                        waits=[ev_src, bank_free[bank]], sig=True)
                    bank_free[bank] = ve(lambda e, i=i, dstt=dstt, bank=bank:
                                         e.tensor_copy(
                        dstt[:, i * TC:(i + 1) * TC],
                        PSM[:, bank * 512: bank * 512 + TC]),
                        waits=[evm], sig=True)
            yield
            for dc in range(DCN):
                par2 = dc % 2
                ps_off = par2 * TC
                ev_dt_mm = pe(lambda e, ps_off=ps_off, dc=dc: e.matmul(
                    PSM[:, ps_off:ps_off + TC],
                    wdtT_sb[:, dc * 128:(dc + 1) * 128],
                    dtp_sb[:], start=True, stop=True, skip_group_check=True),
                    waits=[ev_dtp_sb, bank_free[par2]], sig=True)
                se(lambda e, par2=par2, ps_off=ps_off, dc=dc: e.activation(
                    sp1[par2][:], PSM[:, ps_off:ps_off + TC], AF.Relu,
                    bias=bdt_sb[:, dc:dc + 1]),
                    waits=[ev_dt_mm, ev_sp_free[par2]])
                se(lambda e, par2=par2, ps_off=ps_off, dc=dc: e.activation(
                    sp2[par2][:], PSM[:, ps_off:ps_off + TC], AF.Abs,
                    bias=bdt_sb[:, dc:dc + 1]))
                se(lambda e, par2=par2: e.activation(
                    sp2[par2][:], sp2[par2][:], AF.Exp, scale=-1.0))
                ev_sp = se(lambda e, par2=par2: e.activation(
                    sp2[par2][:], sp2[par2][:], AF.Ln, bias=1.0), sig=True)
                ev_dt = ve(lambda e, par2=par2: e.tensor_tensor(
                    dt_b[par2][:], sp1[par2][:], sp2[par2][:], ALU.add),
                    waits=[ev_sp, ev_ab_last[par2]], sig=True)
                ev_sp_free[par2] = ev_dt
                ve(lambda e, par2=par2, dc=dc: e.tensor_tensor(
                    dtx_b[par2][:], dt_b[par2][:],
                    xnT[:, dc * TC:(dc + 1) * TC], ALU.mult))
                ev_y_last = None
                for i in range(8):
                    n = nh * 8 + i
                    npar = i % 2
                    ev_ab = se(lambda e, par2=par2, npar=npar, n=n, dc=dc:
                               e.activation(
                        ab_b[npar][:], dt_b[par2][:], AF.Exp,
                        scale=a_sb[:, n * 16 + dc: n * 16 + dc + 1]),
                        waits=[ev_dt, ev_a, ev_h_hist[npar]], sig=True)
                    if i == 7:
                        ev_ab_last[par2] = ev_ab
                    ve(lambda e, par2=par2, npar=npar, i=i: e.tensor_tensor(
                        bb_b[npar][:], dtx_b[par2][:],
                        Bb[:, i * TC:(i + 1) * TC], ALU.mult))
                    ev_h = ve(lambda e, npar=npar, n=n, dc=dc:
                              e.tensor_tensor_scan(
                        h_b[npar][:], ab_b[npar][:], bb_b[npar][:],
                        state[:, n * 16 + dc: n * 16 + dc + 1],
                        ALU.mult, ALU.add),
                        waits=[ev_ab], sig=True)
                    ev_h_hist[npar] = ev_h
                    ve(lambda e, npar=npar, n=n, dc=dc: e.tensor_copy(
                        state[:, n * 16 + dc: n * 16 + dc + 1],
                        h_b[npar][:, TC - 1:TC]), waits=[ev_h])
                    ve(lambda e, npar=npar, i=i: e.tensor_tensor(
                        tmp_b[npar][:], h_b[npar][:],
                        Cb[:, i * TC:(i + 1) * TC], ALU.mult))
                    if i == 0 and nh == 0:
                        ve(lambda e, npar=npar, dc=dc, par2=par2:
                           e.scalar_tensor_tensor(
                            y_b[par2][:], xnT[:, dc * TC:(dc + 1) * TC],
                            d_sb[:, dc:dc + 1], tmp_b[npar][:],
                            ALU.mult, ALU.add),
                            waits=[ev_y_free[par2]])
                    elif i == 0:
                        ve(lambda e, npar=npar, par2=par2: e.tensor_copy(
                            y_b[par2][:], tmp_b[npar][:]),
                            waits=[ev_y_free[par2]])
                    else:
                        ev_y_last = ve(lambda e, npar=npar, par2=par2:
                                       e.tensor_tensor(
                            y_b[par2][:], y_b[par2][:], tmp_b[npar][:],
                            ALU.add), sig=(i == 7))
                pending.append((dc, par2, ev_y_last, ev_sp))
                yield
                if len(pending) >= 2:
                    flush_yT()
        while pending:
            flush_yT()
        mstate["ret"] = mstate["last_add"]

    out_ev = [None, None]

    def load(c):
        return sy(lambda e, c=c: e.dma_start(
            x_bufs[c % 2][:].rearrange("p (t d) -> p t d", t=NTT),
            x_in[c * TC:(c + 1) * TC, :].rearrange(
                "(t p) d -> p t d", p=128)), waits=[out_ev[c % 2]], sig=True)

    def store(c, ready):
        ev = sy(lambda e, c=c: e.dma_start(
            y_out[c * TC:(c + 1) * TC, :].rearrange("(t p) d -> p t d", p=128),
            x_bufs[c % 2][:].rearrange("p (t d) -> p t d", t=NTT)),
            waits=[ready], sig=True)
        out_ev[c % 2] = ev
        return ev

    def step(g):
        try:
            next(g)
            return True
        except StopIteration:
            return False

    def run_solo(g):
        while step(g):
            pass

    def run_pair(gmb, gff):
        step(gmb)
        step(gmb)
        acc = 0.0
        mb_live = ff_live = True
        while ff_live or mb_live:
            if ff_live:
                ff_live = step(gff)
            acc += 33.0 / 11.0
            while acc >= 1.0:
                acc -= 1.0
                if mb_live:
                    mb_live = step(gmb)
            if not ff_live and mb_live:
                mb_live = step(gmb)

    ev_out_final = None
    fkey = lambda tag, c: (tag, 0 if use_scratch else c)
    pipelined = (set(phases) == {"f1", "mb", "f2"} and n_chunks == 4)
    if pipelined:
        f1ret = [None] * n_chunks
        mbret = [None] * n_chunks
        evx = load(0)
        run_solo(ffn(evx, fkey("w", 0), 0))
        f1ret[0] = pstate["ret"]
        for c in range(n_chunks):
            gmb = mamba(f1ret[c], c % 2)
            if c + 1 < n_chunks:
                evx = load(c + 1)
                gff = ffn(evx, fkey("w", c + 1), (c + 1) % 2)
                run_pair(gmb, gff)
                mbret[c] = mstate["ret"]
                f1ret[c + 1] = pstate["ret"]
                if c + 2 >= n_chunks:
                    continue
                run_solo(ffn(mbret[c], fkey("w", c + 100), c % 2))
                ev_out_final = store(c, pstate["ret"])
            else:
                gff = ffn(mbret[c - 1], fkey("w", c + 99), (c - 1) % 2)
                run_pair(gmb, gff)
                mbret[c] = mstate["ret"]
                ev_out_final = store(c - 1, pstate["ret"])
                run_solo(ffn(mbret[c], fkey("w", c + 100), c % 2))
                ev_out_final = store(c, pstate["ret"])
    else:
        for c in range(n_chunks):
            ready = load(c)
            if "f1" in phases:
                run_solo(ffn(ready, fkey("w", c), c % 2))
                ready = pstate["ret"]
            if "mb" in phases:
                run_solo(mamba(ready, c % 2))
                ready = mstate["ret"]
            if "f2" in phases:
                run_solo(ffn(ready, fkey("w", c + 100), c % 2))
                ready = pstate["ret"]
            ev_out_final = store(c, ready)

    if debug:
        gp(lambda e: e.dma_start(dbg_xnT[:], xnT_f[:]), waits=[ev_out_final],
           sig=True)
        gp(lambda e: e.dma_start(dbg_lgT[:], lgT[:]), sig=True)
        gp(lambda e: e.dma_start(dbg_comb[:], comb05_p[0][:]), sig=True)
        gp(lambda e: e.dma_start(dbg_h[:], s_sb[:]), sig=True)
        ev_out_final = gp(lambda e: e.dma_start(dbg_st[:], stats[:]), sig=True)
    gp(lambda e: e.engine_nop(), waits=[ev_out_final])

    with nc.Block() as blk:
        S.emit(blk)
    return nc


def _host_inputs(inputs, core):
    x = np.ascontiguousarray(np.asarray(inputs["L0_attn_x"])[core])
    gw = np.asarray(inputs["L0_ffn_gate_w"], dtype=np.float32)
    gwT_full = np.zeros((D, 16), dtype=np.float32)
    gwT_full[:, :8] = gw.T
    gwT = gwT_full.reshape(16, 128, 16).transpose(1, 0, 2).reshape(128, 256)
    alog = np.asarray(inputs["L0_attn_A_log"], dtype=np.float32)
    alog_r = np.ascontiguousarray(
        alog.reshape(16, 128, 16).transpose(1, 2, 0).reshape(128, 256))
    selbc = np.zeros((16, 16 * 128), dtype=np.float32)
    for n in range(16):
        selbc[n, n * 128:(n + 1) * 128] = 1.0
    return dict(
        x_in=x,
        gwT=np.ascontiguousarray(gwT),
        wg=np.ascontiguousarray(np.asarray(inputs["L0_ffn_w_gate"], dtype=np.float32)),
        wu=np.ascontiguousarray(np.asarray(inputs["L0_ffn_w_up"], dtype=np.float32)),
        wd=np.ascontiguousarray(np.asarray(inputs["L0_ffn_w_down"], dtype=np.float32)),
        shg=np.ascontiguousarray(
            np.asarray(inputs["L0_ffn_shgate_w"], dtype=np.float32).reshape(16, 128).T),
        shwg=np.ascontiguousarray(np.asarray(inputs["L0_ffn_shw_gate"], dtype=np.float32)),
        shwu=np.ascontiguousarray(np.asarray(inputs["L0_ffn_shw_up"], dtype=np.float32)),
        shwd=np.ascontiguousarray(np.asarray(inputs["L0_ffn_shw_down"], dtype=np.float32)),
        wxT=np.ascontiguousarray(
            np.asarray(inputs["L0_attn_W_xproj"], dtype=np.float32).T.reshape(
                16, 128, 96).transpose(1, 0, 2).reshape(128, DCN * 96)),
        wdtT=np.ascontiguousarray(np.asarray(inputs["L0_attn_W_dt"], dtype=np.float32).T),
        bdt=np.ascontiguousarray(
            np.asarray(inputs["L0_attn_b_dt"], dtype=np.float32).reshape(16, 128).T),
        alog=alog_r,
        dvec=np.ascontiguousarray(
            np.asarray(inputs["L0_attn_D"], dtype=np.float32).reshape(16, 128).T),
        idf=np.eye(128, dtype=np.float32),
        selbc=selbc,
    )


def kernel(**inputs):
    nc = build_nc()
    in_maps = [_host_inputs(inputs, core) for core in range(8)]
    res = run_bass_kernel_spmd(nc, in_maps, core_ids=list(range(8)))
    out = np.stack([np.asarray(r["y_out"]) for r in res.results], axis=0)
    return out.astype(np.float32)


# version 10
# speedup vs baseline: 1.2286x; 1.2286x over previous
import numpy as np
import ml_dtypes
import concourse.bass as bass
import concourse.mybir as mybir
from concourse.bass_utils import run_bass_kernel_spmd

F32 = mybir.dt.float32
BF16 = mybir.dt.bfloat16
AF = mybir.ActivationFunctionType
ALU = mybir.AluOpType
AX = mybir.AxisListType

L = 2048
D = 2048
E = 8
H = 1408
HS = 2816
RK = 64
NS = 16
TC = 512
NTT = 4
DCN = 16
HCN = 11
EPS = 1e-5

ENGINES = ("pe", "ve", "se", "gp", "sy")
DMA_ENG = ("gp", "sy")


NQ = 12


class Sched:

    def __init__(self, nc):
        self.nc = nc
        self.ops = {e: [] for e in ENGINES}
        self.sems = {e: nc.alloc_semaphore("sem_" + e) for e in ENGINES}
        for q in ("sy", "gp", "se"):
            for i in range(NQ):
                self.sems[(q, i)] = nc.alloc_semaphore(f"dq_{q}{i}")
        self.counts = {k: 0 for k in self.sems}
        self.dma_rr = {"sy": 0, "gp": 0, "se": 0}

    def op(self, eng, fn, waits=(), sig=False, inc=None):
        ev = None
        waits = tuple(w for w in waits if w is not None)
        if sig:
            if inc is None:
                inc = 16 if eng in DMA_ENG else 1
            if inc == 16:
                i = self.dma_rr[eng]
                self.dma_rr[eng] = (i + 1) % NQ
                key = (eng, i)
            else:
                key = eng
            self.counts[key] += inc
            ev = (key, self.counts[key])
        self.ops[eng].append((fn, waits, ev))
        return ev

    def emit(self, blk):
        mk = {"pe": blk.tensor, "ve": blk.vector, "se": blk.scalar,
              "gp": blk.gpsimd, "sy": blk.sync}
        for eng in ENGINES:
            mk[eng](self._mk_body(eng))

    def _mk_body(self, eng):
        sems = self.sems
        ops = self.ops[eng]

        def _body(e):
            last_waits = {}
            prev = {}
            for fn, waits, ev in ops:
                for (wkey, wval) in waits:
                    if last_waits.get(wkey, -1) >= wval:
                        continue
                    e.wait_ge(sems[wkey], wval)
                    last_waits[wkey] = wval
                ins = fn(e)
                if ev is not None:
                    key = ev[0]
                    ins.then_inc(sems[key], ev[1] - prev.get(key, 0))
                    prev[key] = ev[1]
        return _body


def build_nc(n_chunks=4, n_experts=8, use_shared=True, phases=("f1", "mb", "f2"),
             use_scratch=True, debug=False):
    nc = bass.Bass("TRN2", target_bir_lowering=False)

    x_in = nc.dram_tensor("x_in", [L, D], F32, kind="ExternalInput")
    gwT = nc.dram_tensor("gwT", [128, DCN * 16], F32, kind="ExternalInput")
    wg = nc.dram_tensor("wg", [E, D, H], F32, kind="ExternalInput")
    wu = nc.dram_tensor("wu", [E, D, H], F32, kind="ExternalInput")
    wd = nc.dram_tensor("wd", [E, H, D], F32, kind="ExternalInput")
    shg = nc.dram_tensor("shg", [128, DCN], F32, kind="ExternalInput")
    shwg = nc.dram_tensor("shwg", [D, HS], F32, kind="ExternalInput")
    shwu = nc.dram_tensor("shwu", [D, HS], F32, kind="ExternalInput")
    shwd = nc.dram_tensor("shwd", [HS, D], F32, kind="ExternalInput")
    wxT = nc.dram_tensor("wxT", [128, DCN * 96], F32, kind="ExternalInput")
    wdtT = nc.dram_tensor("wdtT", [RK, D], F32, kind="ExternalInput")
    bdt = nc.dram_tensor("bdt", [128, DCN], F32, kind="ExternalInput")
    alog = nc.dram_tensor("alog", [128, 256], F32, kind="ExternalInput")
    dvec = nc.dram_tensor("dvec", [128, DCN], F32, kind="ExternalInput")
    idf = nc.dram_tensor("idf", [128, 128], F32, kind="ExternalInput")
    selbc = nc.dram_tensor("selbc", [16, 16 * 128], F32, kind="ExternalInput")
    y_out = nc.dram_tensor("y_out", [L, D], F32, kind="ExternalOutput")
    if debug:
        dbg_xnT = nc.dram_tensor("dbg_xnT", [128, DCN * TC], BF16,
                                 kind="ExternalOutput")
        dbg_lgT = nc.dram_tensor("dbg_lgT", [128, 64], F32, kind="ExternalOutput")
        dbg_comb = nc.dram_tensor("dbg_comb", [128, 32], F32,
                                  kind="ExternalOutput")
        dbg_h = nc.dram_tensor("dbg_h", [128, HCN * TC], BF16,
                               kind="ExternalOutput")
        dbg_st = nc.dram_tensor("dbg_st", [128, 48], F32, kind="ExternalOutput")
    if use_scratch:
        scr = nc.dram_tensor("scr", [128, 680000], BF16)

    A = nc.alloc_sbuf_tensor
    x_bufs = [A(f"x_sb{i}", [128, NTT * D], F32) for i in range(2)]
    xnT_f = A("xnT_f", [128, DCN * TC], BF16)
    xnT_m = A("xnT_m", [128, DCN * TC], BF16)
    xn_tok = [A(f"xn_tok{i}", [128, D], BF16) for i in range(2)]
    junk = A("junk", [128, D], BF16)
    s_sb = A("s_sb", [128, HCN * TC], BF16)
    guring = [A(f"guring{i}", [128, 1024], BF16) for i in range(8)]
    wdring = [A(f"wdring{i}", [128, 1024], BF16) for i in range(4)]
    Bb = A("Bb", [128, 8 * TC], BF16)
    Cb = A("Cb", [128, 8 * TC], BF16)
    dtp_sb = A("dtp_sb", [RK, TC], BF16)
    B_sb = A("B_sb", [NS, TC], BF16)
    C_sb = A("C_sb", [NS, TC], BF16)
    sp1 = [A(f"sp1_{i}", [128, TC], F32) for i in range(2)]
    sp2 = [A(f"sp2_{i}", [128, TC], F32) for i in range(2)]
    dt_b = [A(f"dt_b{i}", [128, TC], F32) for i in range(2)]
    dtx_b = [A(f"dtx_b{i}", [128, TC], BF16) for i in range(2)]
    ab_b = [A(f"ab_b{i}", [128, TC], F32) for i in range(2)]
    bb_b = [A(f"bb_b{i}", [128, TC], BF16) for i in range(2)]
    h_b = [A(f"h_b{i}", [128, TC], BF16) for i in range(2)]
    tmp_b = [A(f"tmp_b{i}", [128, TC], BF16) for i in range(2)]
    y_b = [A(f"y_b{i}", [128, TC], F32) for i in range(2)]
    state = A("state", [128, 256], F32)
    wdtT_sb = A("wdtT_sb", [RK, D], BF16)
    wxT_sb = A("wxT_sb", [128, DCN * 96], BF16)
    gwT_sb = A("gwT_sb", [128, DCN * 16], BF16)
    selbc_sb = A("selbc_sb", [16, 16 * 128], BF16)
    idf_sb = A("idf_sb", [128, 128], F32)
    a_sb = A("a_sb", [128, 256], F32)
    bdt_sb = A("bdt_sb", [128, DCN], F32)
    d_sb = A("d_sb", [128, DCN], F32)
    shg_sb = A("shg_sb", [128, DCN], BF16)
    stats = A("stats", [128, 48], F32)
    lg_sb = A("lg_sb", [16, TC], F32)
    lgT = A("lgT", [128, NTT * 16], F32)
    comb05_p = [A(f"comb05_{i}", [128, NTT * 8], F32) for i in range(2)]
    sg05_p = [A(f"sg05_{i}", [128, NTT], F32) for i in range(2)]
    msk1 = A("msk1", [128, 8], F32)
    msk2 = A("msk2", [128, 8], F32)
    lg2 = A("lg2", [128, 8], F32)
    tcol = A("tcol", [128, 8], F32)
    cvals = [1.0, -1.0, EPS, -0.5, 0.5]
    cpool = A("cpool", [128, len(cvals)], F32)
    for i, vv in enumerate(cvals):
        nc.const_aps.register(F32, vv, cpool[:, i:i + 1])

    PSG = nc.alloc_psum_tensor("PSG", [128, 1024], F32)
    PSU = nc.alloc_psum_tensor("PSU", [128, 1024], F32)
    PSC = nc.alloc_psum_tensor("PSC", [128, 1024], F32)
    PSM = nc.alloc_psum_tensor("PSM", [128, 1024], F32)

    S = Sched(nc)

    def pe(fn, waits=(), sig=False):
        return S.op("pe", fn, waits, sig)

    def ve(fn, waits=(), sig=False):
        return S.op("ve", fn, waits, sig)

    def se(fn, waits=(), sig=False, inc=None):
        return S.op("se", fn, waits, sig, inc)

    def gp(fn, waits=(), sig=False, inc=None):
        return S.op("gp", fn, waits, sig, inc)

    def sy(fn, waits=(), sig=False):
        return S.op("sy", fn, waits, sig)

    for i, vv in enumerate(cvals):
        ve(lambda e, i=i, vv=vv: e.memset(cpool[:, i:i + 1], vv))
    ve(lambda e: e.memset(state[:], 0.0))
    sy(lambda e: e.dma_start(idf_sb[:], idf[:]), sig=True)
    sy(lambda e: e.dma_start(bdt_sb[:], bdt[:]), sig=True)
    sy(lambda e: e.dma_start(d_sb[:], dvec[:]), sig=True)
    ev_alog = sy(lambda e: e.dma_start(a_sb[:], alog[:]), sig=True)
    gp(lambda e: e.dma_start(wdtT_sb[:], wdtT[:]), sig=True)
    gp(lambda e: e.dma_start(wxT_sb[:], wxT[:]), sig=True)
    gp(lambda e: e.dma_start(gwT_sb[:], gwT[:]), sig=True)
    gp(lambda e: e.dma_start(selbc_sb[:], selbc[:]), sig=True)
    ev_setup_gp = gp(lambda e: e.dma_start(shg_sb[:], shg[:]), sig=True)
    ev_exp = se(lambda e: e.activation(a_sb[:], a_sb[:], AF.Exp),
                waits=[ev_alog], sig=True)
    ev_a = ve(lambda e: e.tensor_single_scalar(a_sb[:], a_sb[:], -1.0, ALU.mult),
              waits=[ev_exp], sig=True)

    ln_evT = [None, None]

    def layer_norm(x_ready, x_sb, xnT, sb):
        ev_sums = None
        for tt in range(NTT):
            ev_sums = ve(lambda e, tt=tt: e.reduce_sum(
                stats[:, sb + tt:sb + tt + 1], x_sb[:, tt * D:(tt + 1) * D], AX.X),
                waits=[x_ready], sig=(tt == NTT - 1))
        ev_sq = None
        for tt in range(NTT):
            ev_sq = se(lambda e, tt=tt: e.activation(
                junk[:], x_sb[:, tt * D:(tt + 1) * D], AF.Square,
                accum_out=stats[:, sb + 4 + tt:sb + 5 + tt]),
                waits=[x_ready], sig=(tt == NTT - 1))
        evg = gp(lambda e: e.tensor_scalar(stats[:, sb + 8:sb + 12], stats[:, sb:sb + 4],
                                           1.0 / D, None, ALU.mult),
                 waits=[ev_sums], sig=True, inc=1)
        evg = gp(lambda e: e.tensor_scalar(stats[:, sb + 16:sb + 20], stats[:, sb + 4:sb + 8],
                                           1.0 / D, None, ALU.mult),
                 waits=[ev_sq, evg], sig=True, inc=1)
        evg = gp(lambda e: e.tensor_tensor(stats[:, sb + 20:sb + 24], stats[:, sb + 8:sb + 12],
                                           stats[:, sb + 8:sb + 12], ALU.mult),
                 waits=[evg], sig=True, inc=1)
        ev_var = gp(lambda e: e.tensor_tensor(stats[:, sb + 12:sb + 16], stats[:, sb + 16:sb + 20],
                                              stats[:, sb + 20:sb + 24], ALU.subtract),
                    waits=[evg], sig=True, inc=1)
        evl = se(lambda e: e.activation(stats[:, sb + 12:sb + 16], stats[:, sb + 12:sb + 16], AF.Ln,
                                        bias=EPS), waits=[ev_var], sig=True)
        ev_rstd = se(lambda e: e.activation(stats[:, sb + 12:sb + 16], stats[:, sb + 12:sb + 16],
                                            AF.Exp, scale=-0.5), waits=[evl],
                     sig=True)
        ev_T = ln_evT
        ev_all = None
        for tt in range(NTT):
            sl = tt % 2
            ev_xn = ve(lambda e, tt=tt, sl=sl: e.tensor_scalar(
                xn_tok[sl][:], x_sb[:, tt * D:(tt + 1) * D],
                stats[:, sb + 8 + tt:sb + 9 + tt], stats[:, sb + 12 + tt:sb + 13 + tt],
                ALU.subtract, ALU.mult),
                waits=[ev_rstd, ev_var, ev_T[sl]], sig=True)
            for dc in range(DCN):
                ev_all = sy(lambda e, tt=tt, sl=sl, dc=dc: e.dma_start(
                    xnT[:, dc * TC + tt * 128: dc * TC + (tt + 1) * 128],
                    xn_tok[sl][:, dc * 128:(dc + 1) * 128], transpose=True),
                    waits=[ev_xn], sig=True)
            ev_T[sl] = ev_all
        return ev_all

    wstate = {"guse": [None] * 8, "guwb": [None] * 8,
              "wdse": [None] * 4, "wdwb": [None] * 4,
              "gu_next": 0, "wd_next": 0, "converted": {}, "scr_off": 0}

    def load_slab(ring, use_l, wb_l, slot, src_f32, key):
        waits = [use_l[slot], wb_l[slot]]
        g, k = src_f32.shape[1], src_f32.shape[2]
        cols = g * k
        dst = ring[slot][:, 0:cols].rearrange("p (g k) -> p g k", g=g)
        if use_scratch:
            if key not in wstate["converted"]:
                off = wstate["scr_off"]
                wstate["scr_off"] = off + cols
                ev = gp(lambda e: e.dma_start(dst, src_f32),
                        waits=waits, sig=True)
                wb = se(lambda e, off=off, cols=cols: e.dma_start(
                    scr[:, off:off + cols], ring[slot][:, 0:cols]),
                    waits=[ev], sig=True, inc=16)
                wstate["converted"][key] = (off, wb)
                wb_l[slot] = wb
                return ev
            off, wb = wstate["converted"][key]
            return sy(lambda e, off=off, cols=cols: e.dma_start(
                ring[slot][:, 0:cols], scr[:, off:off + cols]),
                waits=waits + [wb], sig=True)
        return gp(lambda e: e.dma_start(dst, src_f32),
                  waits=waits, sig=True)

    HQ_E = [(i, 1) for i in range(HCN)]

    pstate = {"psg": None, "psu": None, "psc": [None, None], "ret": None}

    def ffn(x_ready, call_key, par):
        x_sb = x_bufs[par]
        xnT = xnT_f
        comb05 = comb05_p[par]
        sg05 = sg05_p[par]
        psg_war = pstate["psg"]
        psu_war = pstate["psu"]
        psc_war = pstate["psc"]
        ev_xnT_all = layer_norm(x_ready, x_sb, xnT, 0)

        ev_lgmm = None
        for dc in range(DCN):
            ev_lgmm = pe(lambda e, dc=dc: e.matmul(
                PSC[0:16, 0:TC], gwT_sb[:, dc * 16:(dc + 1) * 16],
                xnT[:, dc * TC:(dc + 1) * TC],
                start=(dc == 0), stop=(dc == DCN - 1), skip_group_check=True),
                waits=[ev_xnT_all, ev_setup_gp, psc_war[0]],
                sig=(dc == DCN - 1))
        ev_lgcp = ve(lambda e: e.tensor_copy(lg_sb[:], PSC[0:16, 0:TC]),
                     waits=[ev_lgmm], sig=True)
        ev_lgT = None
        for tt in range(NTT):
            ev_lgT = pe(lambda e, tt=tt: e.transpose(
                PSC[:, TC + tt * 16: TC + (tt + 1) * 16],
                lg_sb[:, tt * 128:(tt + 1) * 128], idf_sb[0:16, 0:16]),
                waits=[ev_lgcp, psc_war[1]], sig=(tt == NTT - 1))
        ev_lgTc = ve(lambda e: e.tensor_copy(
            lgT[:, 0:NTT * 16], PSC[:, TC:TC + NTT * 16]),
            waits=[ev_lgT], sig=True)

        ev_sgmm = None
        for tt in range(NTT):
            for dc in range(DCN):
                ev_sgmm = pe(lambda e, tt=tt, dc=dc: e.matmul(
                    PSC[:, tt:tt + 1],
                    xnT[:, dc * TC + tt * 128: dc * TC + (tt + 1) * 128],
                    shg_sb[:, dc:dc + 1],
                    start=(tt == 0 and dc == 0),
                    stop=(tt == NTT - 1 and dc == DCN - 1),
                    skip_group_check=True),
                    waits=[ev_xnT_all, ev_lgcp],
                    sig=(tt == NTT - 1 and dc == DCN - 1))
        evsg = se(lambda e: e.activation(sg05[:], PSC[:, 0:NTT],
                                         AF.Sigmoid), waits=[ev_sgmm], sig=True)
        ev_sg05 = se(lambda e: e.mul(sg05[:], sg05[:], 0.5), waits=[evsg],
                     sig=True)

        ev_comb = None
        ev4_prev = None
        for tt in range(NTT):
            lgt8 = lgT[:, tt * 16: tt * 16 + 8]
            c0 = tcol[:, 0:1]
            c1 = tcol[:, 1:2]
            c2 = tcol[:, 2:3]
            c34 = tcol[:, 3:5]
            evp = ve(lambda e, lgt8=lgt8, c0=c0: e.reduce_max(c0, lgt8, AX.X),
                     waits=[ev_lgTc], sig=True)
            evp = ve(lambda e, lgt8=lgt8, c0=c0: e.tensor_scalar(
                msk1[:], lgt8, c0, None, ALU.is_equal), waits=[evp], sig=True)
            evp = ve(lambda e, lgt8=lgt8: e.scalar_tensor_tensor(
                lg2[:], msk1[:], -1e30, lgt8, ALU.mult, ALU.add),
                waits=[evp], sig=True)
            evp = ve(lambda e, c1=c1: e.reduce_max(c1, lg2[:], AX.X),
                     waits=[evp], sig=True)
            ev3 = ve(lambda e, c0=c0, c1=c1, c2=c2: e.tensor_tensor(
                c2, c0, c1, ALU.subtract), waits=[evp, ev4_prev], sig=True)
            se(lambda e, c2=c2, c34=c34: e.activation(c34[:, 0:1], c2, AF.Sigmoid),
               waits=[ev3])
            ev4a = se(lambda e, c2=c2, c34=c34: e.activation(
                c34[:, 1:2], c2, AF.Sigmoid, scale=-1.0), sig=True)
            ev4 = se(lambda e, c34=c34: e.mul(c34, c34, 0.5), waits=[ev4a],
                     sig=True)
            ev4_prev = ev4
            evp = ve(lambda e, c1=c1: e.tensor_scalar(
                msk2[:], lg2[:], c1, None, ALU.subtract), waits=[ev4, ev3],
                sig=True)
            evp = ve(lambda e: e.tensor_single_scalar(msk2[:], msk2[:], 0.0,
                                                      ALU.is_equal),
                     waits=[evp], sig=True)
            evp = ve(lambda e, c34=c34, tt=tt: e.tensor_scalar(
                comb05[:, tt * 8:(tt + 1) * 8], msk1[:], c34[:, 0:1], None,
                ALU.mult), waits=[evp], sig=True)
            ev_comb = ve(lambda e, c34=c34, tt=tt: e.scalar_tensor_tensor(
                comb05[:, tt * 8:(tt + 1) * 8], msk2[:], c34[:, 1:2],
                comb05[:, tt * 8:(tt + 1) * 8], ALU.mult, ALU.add),
                waits=[evp], sig=True)

        yield
        ev_PSG_free = list(psg_war) if psg_war else [None, None]
        ev_PSU_free = list(psu_war) if psu_war else [None, None]
        ev_PSC_free = [ev_sg05, ev_lgTc]

        def gu_pass(srcs, h_groups, skey):
            nonlocal ev_PSG_free, ev_PSU_free
            evs_h = []
            for gi, (h0, hn) in enumerate(h_groups):
                bank = gi % 2
                poff = bank * TC
                slabs = {}
                for dg in range(2):
                    for kind in range(2):
                        slot = wstate["gu_next"]
                        wstate["gu_next"] = (slot + 1) % 8
                        ev = load_slab(
                            guring, wstate["guse"], wstate["guwb"], slot,
                            srcs[kind](dg, h0, hn),
                            key=(skey, gi, dg, kind))
                        slabs[(dg, kind)] = (slot, ev)
                ev_g = ev_u = None
                for dc in range(DCN):
                    dg, di = dc // 8, dc % 8
                    for kind in range(2):
                        slot, ev_ld = slabs[(dg, kind)]
                        tgt = PSG if kind == 0 else PSU
                        war = (ev_PSG_free if kind == 0 else ev_PSU_free)[bank]
                        evm = pe(lambda e, slot=slot, di=di, dc=dc,
                                 tgt=tgt, poff=poff: e.matmul(
                            tgt[:, poff:poff + TC],
                            guring[slot][:, di * 128:(di + 1) * 128],
                            xnT[:, dc * TC:(dc + 1) * TC],
                            start=(dc == 0), stop=(dc == DCN - 1),
                            skip_group_check=True),
                            waits=[ev_ld, ev_xnT_all,
                                   war if dc == 0 else None],
                            sig=(dc == dg * 8 + 7))
                        if evm is not None:
                            if kind == 0:
                                ev_g = evm
                            else:
                                ev_u = evm
                            wstate["guse"][slot] = evm
                ev_s = se(lambda e, h0=h0, poff=poff: e.activation(
                    s_sb[:, (h0 % HCN) * TC:((h0 % HCN) + 1) * TC],
                    PSG[:, poff:poff + TC], AF.Silu), waits=[ev_g], sig=True)
                ev_PSG_free[bank] = ev_s
                ev_hm = ve(lambda e, h0=h0, poff=poff: e.tensor_tensor(
                    s_sb[:, (h0 % HCN) * TC:((h0 % HCN) + 1) * TC],
                    s_sb[:, (h0 % HCN) * TC:((h0 % HCN) + 1) * TC],
                    PSU[:, poff:poff + TC], ALU.mult),
                    waits=[ev_s, ev_u], sig=True)
                ev_PSU_free[bank] = ev_hm
                evs_h.append(ev_hm)
            return evs_h

        def down_pass(src_fn, nhc, evs_h, skey, evac_fn):
            nonlocal ev_PSC_free
            ng = (nhc + 1) // 2
            for db in range(4):
                for ttp in range(2):
                    ev_dmm = None
                    for hc2 in range(ng):
                        hc0 = hc2 * 2
                        hcnt = 2 if hc0 + 2 <= nhc else 1
                        slot = wstate["wd_next"]
                        wstate["wd_next"] = (slot + 1) % 4
                        ev_ld = load_slab(
                            wdring, wstate["wdse"], wstate["wdwb"], slot,
                            src_fn(hc0, hcnt, db),
                            key=(skey, db, hc2))
                        ev_slab = None
                        for hl in range(hcnt):
                            hc = hc0 + hl
                            for tl in range(2):
                                tt = ttp * 2 + tl
                                evm = pe(lambda e, slot=slot, hl=hl, hc=hc,
                                         tt=tt, tl=tl, nhc=nhc: e.matmul(
                                    PSC[:, tl * 512:(tl + 1) * 512],
                                    s_sb[:, hc * TC + tt * 128:
                                         hc * TC + (tt + 1) * 128],
                                    wdring[slot][:, hl * 512:(hl + 1) * 512],
                                    start=(hc == 0), stop=(hc == nhc - 1),
                                    skip_group_check=True),
                                    waits=[ev_ld, evs_h[hc],
                                           ev_PSC_free[tl] if hc == 0 else None],
                                    sig=(hl == hcnt - 1 and tl == 1))
                                if evm is not None:
                                    ev_slab = evm
                        wstate["wdse"][slot] = ev_slab
                        ev_dmm = ev_slab
                    for tl in range(2):
                        ev_PSC_free[tl] = evac_fn(ttp * 2 + tl, tl, db, ev_dmm)

        for ex in range(n_experts):
            def gsrc(dg, h0, hn, ex=ex):
                return wg[ex, dg * 1024:(dg + 1) * 1024,
                          h0 * 128:(h0 + hn) * 128].rearrange(
                    "(g p) k -> p g k", p=128)

            def usrc(dg, h0, hn, ex=ex):
                return wu[ex, dg * 1024:(dg + 1) * 1024,
                          h0 * 128:(h0 + hn) * 128].rearrange(
                    "(g p) k -> p g k", p=128)
            evs_h = gu_pass((gsrc, usrc), HQ_E, (call_key, "gu", ex))

            def dsrc(hc0, hcnt, db, ex=ex):
                return wd[ex, hc0 * 128:(hc0 + hcnt) * 128,
                          db * 512:(db + 1) * 512].rearrange(
                    "(g p) k -> p g k", p=128)
            def evac(tt, tl, db, ev_dmm, ex=ex):
                return ve(lambda e, tt=tt, tl=tl, db=db, ex=ex:
                          e.scalar_tensor_tensor(
                    x_sb[:, tt * D + db * 512: tt * D + (db + 1) * 512],
                    PSC[:, tl * 512:(tl + 1) * 512],
                    comb05[:, tt * 8 + ex: tt * 8 + ex + 1],
                    x_sb[:, tt * D + db * 512: tt * D + (db + 1) * 512],
                    ALU.mult, ALU.add),
                    waits=[ev_dmm, ev_comb], sig=True)
            down_pass(dsrc, HCN, evs_h, (call_key, "wd", ex), evac)
            yield

        if use_shared:
            for half in range(2):
                hbase = half * 11

                def gsrc(dg, h0, hn):
                    return shwg[dg * 1024:(dg + 1) * 1024,
                                h0 * 128:(h0 + hn) * 128].rearrange(
                        "(g p) k -> p g k", p=128)

                def usrc(dg, h0, hn):
                    return shwu[dg * 1024:(dg + 1) * 1024,
                                h0 * 128:(h0 + hn) * 128].rearrange(
                        "(g p) k -> p g k", p=128)
                groups = [(hbase + a, b) for (a, b) in HQ_E]
                evs_h = gu_pass((gsrc, usrc), groups,
                                (call_key, "shgu", half))

                def dsrc(hc0, hcnt, db, hbase=hbase):
                    hh0 = hbase + hc0
                    return shwd[hh0 * 128:(hh0 + hcnt) * 128,
                                db * 512:(db + 1) * 512].rearrange(
                        "(g p) k -> p g k", p=128)
                def evac(tt, tl, db, ev_dmm):
                    return ve(lambda e, tt=tt, tl=tl, db=db:
                              e.scalar_tensor_tensor(
                        x_sb[:, tt * D + db * 512: tt * D + (db + 1) * 512],
                        PSC[:, tl * 512:(tl + 1) * 512],
                        sg05[:, tt:tt + 1],
                        x_sb[:, tt * D + db * 512: tt * D + (db + 1) * 512],
                        ALU.mult, ALU.add),
                        waits=[ev_dmm, ev_sg05], sig=True)
                down_pass(dsrc, 11, evs_h, (call_key, "shwd", half), evac)
                if half == 0:
                    yield
        pstate["psg"] = ev_PSG_free
        pstate["psu"] = ev_PSU_free
        pstate["psc"] = ev_PSC_free
        pstate["ret"] = ev_PSC_free[1]

    mstate = {"ret": None}

    def mamba(x_ready, par):
        x_sb = x_bufs[par]
        xnT = xnT_m
        ev_xnT_all = layer_norm(x_ready, x_sb, xnT, 24)
        yield
        ev_dtp = ev_B = ev_C = None
        for dc in range(DCN):
            ev_dtp = pe(lambda e, dc=dc: e.matmul(
                PSM[0:64, 0:TC], wxT_sb[:, dc * 96: dc * 96 + 64],
                xnT[:, dc * TC:(dc + 1) * TC], start=(dc == 0),
                stop=(dc == DCN - 1), skip_group_check=True),
                waits=[ev_xnT_all, ev_setup_gp], sig=(dc == DCN - 1))
        for dc in range(DCN):
            ev_B = pe(lambda e, dc=dc: e.matmul(
                PSM[0:16, TC:2 * TC], wxT_sb[:, dc * 96 + 64: dc * 96 + 80],
                xnT[:, dc * TC:(dc + 1) * TC], start=(dc == 0),
                stop=(dc == DCN - 1), skip_group_check=True),
                sig=(dc == DCN - 1))
        ev_dtp_sb = ve(lambda e: e.tensor_copy(dtp_sb[:], PSM[0:64, 0:TC]),
                       waits=[ev_dtp], sig=True)
        ev_B_sb = ve(lambda e: e.tensor_copy(B_sb[:], PSM[0:16, TC:2 * TC]),
                     waits=[ev_B], sig=True)
        for dc in range(DCN):
            ev_C = pe(lambda e, dc=dc: e.matmul(
                PSM[0:16, TC:2 * TC], wxT_sb[:, dc * 96 + 80: dc * 96 + 96],
                xnT[:, dc * TC:(dc + 1) * TC], start=(dc == 0),
                stop=(dc == DCN - 1), skip_group_check=True),
                waits=[ev_B_sb if dc == 0 else None], sig=(dc == DCN - 1))
        ev_C_sb = ve(lambda e: e.tensor_copy(C_sb[:], PSM[0:16, TC:2 * TC]),
                     waits=[ev_C], sig=True)
        pending = []

        def flush_yT():
            dc, par2, ev_y_last, ev_sp = pending.pop(0)
            tb = par2 * 512
            ev_yT = None
            for tt in range(NTT):
                ev_yT = pe(lambda e, tt=tt, par2=par2, tb=tb: e.transpose(
                    PSM[:, tb + tt * 128: tb + (tt + 1) * 128],
                    y_b[par2][:, tt * 128:(tt + 1) * 128], idf_sb[:]),
                    waits=[ev_y_last, ev_sp], sig=(tt == NTT - 1))
            ev_add = None
            for tt in range(NTT):
                ev_add = ve(lambda e, tt=tt, dc=dc, tb=tb: e.tensor_tensor(
                    x_sb[:, tt * D + dc * 128: tt * D + (dc + 1) * 128],
                    x_sb[:, tt * D + dc * 128: tt * D + (dc + 1) * 128],
                    PSM[:, tb + tt * 128: tb + (tt + 1) * 128], ALU.add),
                    waits=[ev_yT], sig=(tt == NTT - 1))
            ev_y_free[par2] = ev_add
            bank_free[par2] = ev_add
            mstate["last_add"] = ev_add

        bank_free = [ev_dtp_sb, ev_C_sb]
        ev_sp_free = [None, None]
        ev_ab_last = [None, None]
        ev_y_free = [None, None]
        ev_h_hist = [None, None]
        ev_final = None
        for nh in range(2):
            for i in range(8):
                n = nh * 8 + i
                for kind in range(2):
                    srcb = B_sb if kind == 0 else C_sb
                    dstt = Bb if kind == 0 else Cb
                    ev_src = ev_B_sb if kind == 0 else ev_C_sb
                    bank = kind
                    evm = pe(lambda e, n=n, srcb=srcb, bank=bank: e.matmul(
                        PSM[:, bank * 512: bank * 512 + TC],
                        selbc_sb[:, n * 128:(n + 1) * 128], srcb[:],
                        start=True, stop=True, skip_group_check=True),
                        waits=[ev_src, bank_free[bank]], sig=True)
                    bank_free[bank] = ve(lambda e, i=i, dstt=dstt, bank=bank:
                                         e.tensor_copy(
                        dstt[:, i * TC:(i + 1) * TC],
                        PSM[:, bank * 512: bank * 512 + TC]),
                        waits=[evm], sig=True)
            yield
            for dc in range(DCN):
                par2 = dc % 2
                ps_off = par2 * TC
                ev_dt_mm = pe(lambda e, ps_off=ps_off, dc=dc: e.matmul(
                    PSM[:, ps_off:ps_off + TC],
                    wdtT_sb[:, dc * 128:(dc + 1) * 128],
                    dtp_sb[:], start=True, stop=True, skip_group_check=True),
                    waits=[ev_dtp_sb, bank_free[par2]], sig=True)
                se(lambda e, par2=par2, ps_off=ps_off, dc=dc: e.activation(
                    sp1[par2][:], PSM[:, ps_off:ps_off + TC], AF.Relu,
                    bias=bdt_sb[:, dc:dc + 1]),
                    waits=[ev_dt_mm, ev_sp_free[par2]])
                se(lambda e, par2=par2, ps_off=ps_off, dc=dc: e.activation(
                    sp2[par2][:], PSM[:, ps_off:ps_off + TC], AF.Abs,
                    bias=bdt_sb[:, dc:dc + 1]))
                se(lambda e, par2=par2: e.activation(
                    sp2[par2][:], sp2[par2][:], AF.Exp, scale=-1.0))
                ev_sp = se(lambda e, par2=par2: e.activation(
                    sp2[par2][:], sp2[par2][:], AF.Ln, bias=1.0), sig=True)
                ev_dt = ve(lambda e, par2=par2: e.tensor_tensor(
                    dt_b[par2][:], sp1[par2][:], sp2[par2][:], ALU.add),
                    waits=[ev_sp, ev_ab_last[par2]], sig=True)
                ev_sp_free[par2] = ev_dt
                ve(lambda e, par2=par2, dc=dc: e.tensor_tensor(
                    dtx_b[par2][:], dt_b[par2][:],
                    xnT[:, dc * TC:(dc + 1) * TC], ALU.mult))
                ev_y_last = None
                for i in range(8):
                    n = nh * 8 + i
                    npar = i % 2
                    ev_ab = se(lambda e, par2=par2, npar=npar, n=n, dc=dc:
                               e.activation(
                        ab_b[npar][:], dt_b[par2][:], AF.Exp,
                        scale=a_sb[:, n * 16 + dc: n * 16 + dc + 1]),
                        waits=[ev_dt, ev_a, ev_h_hist[npar]], sig=True)
                    if i == 7:
                        ev_ab_last[par2] = ev_ab
                    ve(lambda e, par2=par2, npar=npar, i=i: e.tensor_tensor(
                        bb_b[npar][:], dtx_b[par2][:],
                        Bb[:, i * TC:(i + 1) * TC], ALU.mult))
                    ev_h = ve(lambda e, npar=npar, n=n, dc=dc:
                              e.tensor_tensor_scan(
                        h_b[npar][:], ab_b[npar][:], bb_b[npar][:],
                        state[:, n * 16 + dc: n * 16 + dc + 1],
                        ALU.mult, ALU.add),
                        waits=[ev_ab], sig=True)
                    ev_h_hist[npar] = ev_h
                    ve(lambda e, npar=npar, n=n, dc=dc: e.tensor_copy(
                        state[:, n * 16 + dc: n * 16 + dc + 1],
                        h_b[npar][:, TC - 1:TC]), waits=[ev_h])
                    ve(lambda e, npar=npar, i=i: e.tensor_tensor(
                        tmp_b[npar][:], h_b[npar][:],
                        Cb[:, i * TC:(i + 1) * TC], ALU.mult))
                    if i == 0 and nh == 0:
                        ve(lambda e, npar=npar, dc=dc, par2=par2:
                           e.scalar_tensor_tensor(
                            y_b[par2][:], xnT[:, dc * TC:(dc + 1) * TC],
                            d_sb[:, dc:dc + 1], tmp_b[npar][:],
                            ALU.mult, ALU.add),
                            waits=[ev_y_free[par2]])
                    elif i == 0:
                        ve(lambda e, npar=npar, par2=par2: e.tensor_copy(
                            y_b[par2][:], tmp_b[npar][:]),
                            waits=[ev_y_free[par2]])
                    else:
                        ev_y_last = ve(lambda e, npar=npar, par2=par2:
                                       e.tensor_tensor(
                            y_b[par2][:], y_b[par2][:], tmp_b[npar][:],
                            ALU.add), sig=(i == 7))
                pending.append((dc, par2, ev_y_last, ev_sp))
                yield
                if len(pending) >= 2:
                    flush_yT()
        while pending:
            flush_yT()
        mstate["ret"] = mstate["last_add"]

    out_ev = [None, None]

    def load(c):
        return sy(lambda e, c=c: e.dma_start(
            x_bufs[c % 2][:].rearrange("p (t d) -> p t d", t=NTT),
            x_in[c * TC:(c + 1) * TC, :].rearrange(
                "(t p) d -> p t d", p=128)), waits=[out_ev[c % 2]], sig=True)

    def store(c, ready):
        ev = sy(lambda e, c=c: e.dma_start(
            y_out[c * TC:(c + 1) * TC, :].rearrange("(t p) d -> p t d", p=128),
            x_bufs[c % 2][:].rearrange("p (t d) -> p t d", t=NTT)),
            waits=[ready], sig=True)
        out_ev[c % 2] = ev
        return ev

    def step(g):
        try:
            next(g)
            return True
        except StopIteration:
            return False

    def run_solo(g):
        while step(g):
            pass

    def run_pair(gmb, gff):
        step(gmb)
        step(gmb)
        acc = 0.0
        mb_live = ff_live = True
        while ff_live or mb_live:
            if ff_live:
                ff_live = step(gff)
            acc += 33.0 / 11.0
            while acc >= 1.0:
                acc -= 1.0
                if mb_live:
                    mb_live = step(gmb)
            if not ff_live and mb_live:
                mb_live = step(gmb)

    ev_out_final = None
    fkey = lambda tag, c: (tag, 0 if use_scratch else c)
    pipelined = (set(phases) == {"f1", "mb", "f2"} and n_chunks == 4)
    if pipelined:
        f1ret = [None] * n_chunks
        mbret = [None] * n_chunks
        evx = load(0)
        run_solo(ffn(evx, fkey("w", 0), 0))
        f1ret[0] = pstate["ret"]
        for c in range(n_chunks):
            gmb = mamba(f1ret[c], c % 2)
            if c + 1 < n_chunks:
                evx = load(c + 1)
                gff = ffn(evx, fkey("w", c + 1), (c + 1) % 2)
                run_pair(gmb, gff)
                mbret[c] = mstate["ret"]
                f1ret[c + 1] = pstate["ret"]
                if c + 2 >= n_chunks:
                    continue
                run_solo(ffn(mbret[c], fkey("w", c + 100), c % 2))
                ev_out_final = store(c, pstate["ret"])
            else:
                gff = ffn(mbret[c - 1], fkey("w", c + 99), (c - 1) % 2)
                run_pair(gmb, gff)
                mbret[c] = mstate["ret"]
                ev_out_final = store(c - 1, pstate["ret"])
                run_solo(ffn(mbret[c], fkey("w", c + 100), c % 2))
                ev_out_final = store(c, pstate["ret"])
    else:
        for c in range(n_chunks):
            ready = load(c)
            if "f1" in phases:
                run_solo(ffn(ready, fkey("w", c), c % 2))
                ready = pstate["ret"]
            if "mb" in phases:
                run_solo(mamba(ready, c % 2))
                ready = mstate["ret"]
            if "f2" in phases:
                run_solo(ffn(ready, fkey("w", c + 100), c % 2))
                ready = pstate["ret"]
            ev_out_final = store(c, ready)

    if debug:
        gp(lambda e: e.dma_start(dbg_xnT[:], xnT_f[:]), waits=[ev_out_final],
           sig=True)
        gp(lambda e: e.dma_start(dbg_lgT[:], lgT[:]), sig=True)
        gp(lambda e: e.dma_start(dbg_comb[:], comb05_p[0][:]), sig=True)
        gp(lambda e: e.dma_start(dbg_h[:], s_sb[:]), sig=True)
        ev_out_final = gp(lambda e: e.dma_start(dbg_st[:], stats[:]), sig=True)
    gp(lambda e: e.engine_nop(), waits=[ev_out_final])

    with nc.Block() as blk:
        S.emit(blk)
    return nc


def _host_inputs(inputs, core):
    x = np.ascontiguousarray(np.asarray(inputs["L0_attn_x"])[core])
    gw = np.asarray(inputs["L0_ffn_gate_w"], dtype=np.float32)
    gwT_full = np.zeros((D, 16), dtype=np.float32)
    gwT_full[:, :8] = gw.T
    gwT = gwT_full.reshape(16, 128, 16).transpose(1, 0, 2).reshape(128, 256)
    alog = np.asarray(inputs["L0_attn_A_log"], dtype=np.float32)
    alog_r = np.ascontiguousarray(
        alog.reshape(16, 128, 16).transpose(1, 2, 0).reshape(128, 256))
    selbc = np.zeros((16, 16 * 128), dtype=np.float32)
    for n in range(16):
        selbc[n, n * 128:(n + 1) * 128] = 1.0
    return dict(
        x_in=x,
        gwT=np.ascontiguousarray(gwT),
        wg=np.ascontiguousarray(np.asarray(inputs["L0_ffn_w_gate"], dtype=np.float32)),
        wu=np.ascontiguousarray(np.asarray(inputs["L0_ffn_w_up"], dtype=np.float32)),
        wd=np.ascontiguousarray(np.asarray(inputs["L0_ffn_w_down"], dtype=np.float32)),
        shg=np.ascontiguousarray(
            np.asarray(inputs["L0_ffn_shgate_w"], dtype=np.float32).reshape(16, 128).T),
        shwg=np.ascontiguousarray(np.asarray(inputs["L0_ffn_shw_gate"], dtype=np.float32)),
        shwu=np.ascontiguousarray(np.asarray(inputs["L0_ffn_shw_up"], dtype=np.float32)),
        shwd=np.ascontiguousarray(np.asarray(inputs["L0_ffn_shw_down"], dtype=np.float32)),
        wxT=np.ascontiguousarray(
            np.asarray(inputs["L0_attn_W_xproj"], dtype=np.float32).T.reshape(
                16, 128, 96).transpose(1, 0, 2).reshape(128, DCN * 96)),
        wdtT=np.ascontiguousarray(np.asarray(inputs["L0_attn_W_dt"], dtype=np.float32).T),
        bdt=np.ascontiguousarray(
            np.asarray(inputs["L0_attn_b_dt"], dtype=np.float32).reshape(16, 128).T),
        alog=alog_r,
        dvec=np.ascontiguousarray(
            np.asarray(inputs["L0_attn_D"], dtype=np.float32).reshape(16, 128).T),
        idf=np.eye(128, dtype=np.float32),
        selbc=selbc,
    )


def kernel(**inputs):
    nc = build_nc()
    in_maps = [_host_inputs(inputs, core) for core in range(8)]
    res = run_bass_kernel_spmd(nc, in_maps, core_ids=list(range(8)))
    out = np.stack([np.asarray(r["y_out"]) for r in res.results], axis=0)
    return out.astype(np.float32)
